# version 05
# baseline (speedup 1.0000x reference)
import numpy as np
from contextlib import ExitStack

import concourse.bass as bass
import concourse.mybir as mybir
from concourse.bass_utils import run_bass_kernel_spmd

F32 = mybir.dt.float32
BF16 = mybir.dt.bfloat16
FP8 = mybir.dt.float8e4
AF = mybir.ActivationFunctionType
ALU = mybir.AluOpType

D = 2048
DT = 16
TCORE = 2048
TC = 512
NCH = 4
ET = 16
IC = 48
HC = 16
EPS_LN = 1e-5
EPS_WKV = 1e-8

SEMS = (["t", "v", "s", "g", "vld", "pk", "pv", "pr", "pg", "pu", "pd",
         "gt", "cci", "bnd", "dbg"]
        + [f"{n}{p}" for n in ("sx", "slk", "slv", "slr", "spn", "spd", "spr",
                               "xb", "gs", "us", "ds", "bn", "bd", "br", "ou")
           for p in range(2)])


class Sch:

    def __init__(self, nc, eng, sems, marks):
        self.nc = nc
        self.which = eng
        self.sems = sems
        self.M = marks
        self.dry = eng is None
        self.cnt = {k: 0 for k in sems}

    def do(self, eng, fn, incs=()):
        inst = None
        if self.which == eng:
            inst = fn(getattr(self.nc, eng))
        for s, k in incs:
            self.cnt[s] += k
            if inst is not None:
                inst.then_inc(self.sems[s], k)
        return inst

    def dma(self, eng, sem, fn):
        return self.do(eng, fn, [(sem, 16)])

    def wait(self, eng, s, val):
        if val and self.which == eng:
            getattr(self.nc, eng).wait_ge(self.sems[s], val)

    def mark(self, name, s):
        v = self.cnt[s]
        if self.dry:
            self.M[name] = v
        else:
            assert self.M[name] == v, (name, self.M[name], v)

    def val(self, name):
        return self.M.get(name, 0)


def build_program(debug=False):
    nc = bass.Bass("TRN2", target_bir_lowering=False)

    xin = nc.dram_tensor("xin", [D, TCORE + 1], F32, kind="ExternalInput").ap()
    wkT = nc.dram_tensor("wkT", [D, D], F32, kind="ExternalInput").ap()
    wvT = nc.dram_tensor("wvT", [D, D], F32, kind="ExternalInput").ap()
    wrT = nc.dram_tensor("wrT", [D, D], F32, kind="ExternalInput").ap()
    gw = nc.dram_tensor("gw", [D, 6144], F32, kind="ExternalInput").ap()
    uw = nc.dram_tensor("uw", [D, 6144], F32, kind="ExternalInput").ap()
    dwn = nc.dram_tensor("dwn", [6144, D], F32, kind="ExternalInput").ap()
    td_col = nc.dram_tensor("td_col", [128, 16], F32, kind="ExternalInput").ap()
    gs_col = nc.dram_tensor("gs_col", [128, 48], F32, kind="ExternalInput").ap()
    us_col = nc.dram_tensor("us_col", [128, 48], F32, kind="ExternalInput").ap()
    dsc_col = nc.dram_tensor("dsc_col", [128, 16], F32, kind="ExternalInput").ap()
    cmask = nc.dram_tensor("cmask", [128, 1], F32, kind="ExternalInput").ap()

    xout = nc.dram_tensor("xout", [D, TCORE], F32, kind="ExternalOutput").ap()

    scr_k = nc.dram_tensor("scr_k", [ET, 128, DT, 128], BF16, kind="Internal").ap()
    scr_v = nc.dram_tensor("scr_v", [ET, 128, DT, 128], BF16, kind="Internal").ap()
    scr_r = nc.dram_tensor("scr_r", [ET, 128, DT, 128], BF16, kind="Internal").ap()
    scr_g = nc.dram_tensor("scr_g", [IC, 128, DT, 128], FP8, kind="Internal").ap()
    scr_u = nc.dram_tensor("scr_u", [IC, 128, DT, 128], FP8, kind="Internal").ap()
    scr_d = nc.dram_tensor("scr_d", [HC, 128, 48, 128], FP8, kind="Internal").ap()
    spk = "ExternalOutput" if debug else "Internal"
    sp_num = nc.dram_tensor("sp_num", [ET, 128, TCORE], F32, kind=spk).ap()
    sp_den = nc.dram_tensor("sp_den", [ET, 128, TCORE], F32, kind=spk).ap()
    sp_r = nc.dram_tensor("sp_r", [ET, 128, TCORE], F32, kind=spk).ap()
    cc_in = nc.dram_tensor("cc_in", [128, 32], F32, kind="Internal").ap()
    cc_out = nc.dram_tensor("cc_out", [256, 32], F32, kind="Internal").ap()
    bnd_dram = nc.dram_tensor("bnd_dram", [1, 2], F32, kind="Internal").ap()
    if debug:
        dbg_xn = nc.dram_tensor("dbg_xn", [128, DT, TCORE + 4], BF16,
                                kind="ExternalOutput").ap()
        dbg_gath = nc.dram_tensor("dbg_gath", [128, 32], F32,
                                  kind="ExternalOutput").ap()
        dbg_x1 = nc.dram_tensor("dbg_x1", [NCH, 128, DT, TC], F32,
                                kind="ExternalOutput").ap()
        dbg_xn2 = nc.dram_tensor("dbg_xn2", [NCH, 128, DT, TC], BF16,
                                 kind="ExternalOutput").ap()

    es = ExitStack()

    def sb(name, shape, dtype):
        return es.enter_context(nc.sbuf_tensor(name, shape, dtype))

    ones_bf = sb("ones_bf", [128, 1], BF16)
    onesrow = sb("onesrow", [1, 128], F32)
    iot = sb("iot", [128, TC], F32)
    s_td = sb("s_td", [128, 16], F32)
    lam = sb("lam", [128, 16], F32)
    neglam = sb("neglam", [128, 16], F32)
    wcol = sb("wcol", [128, 16], F32)
    nlb = sb("nlb", [128, 16], F32)
    s_gs = sb("s_gs", [128, 48], F32)
    s_us = sb("s_us", [128, 48], F32)
    s_dsc = sb("s_dsc", [128, 16], F32)
    s_cm = sb("s_cm", [128, 1], F32)
    epsb = sb("epsb", [128, 1], F32)
    bndcol = sb("bndcol", [128, 2], F32)
    xbb = sb("xbb", [128, 16], BF16)
    xbb2 = sb("xbb2", [128, 16], BF16)
    stt = sb("stt", [128, 32], F32)
    gath = sb("gath", [128, 32], F32)
    ninb = sb("ninb", [128, 16], F32)
    dinb = sb("dinb", [128, 16], F32)
    r_mean = sb("r_mean", [1, TC], F32)
    r_var = sb("r_var", [1, TC], F32)
    r_rstd = sb("r_rstd", [1, TC], F32)
    tmpf = sb("tmpf", [128, TC], F32)

    esX = ExitStack()
    xn = esX.enter_context(nc.sbuf_tensor("xn", [128, DT, TCORE + 4], BF16))
    esA1 = ExitStack()
    sx = [esA1.enter_context(nc.sbuf_tensor(f"sx{p}", [128, DT, TC + 1], F32))
          for p in range(2)]
    xb = esA1.enter_context(nc.sbuf_tensor("xb", [128, DT, TC], BF16))
    xb2 = esA1.enter_context(nc.sbuf_tensor("xb2", [128, DT, TC], BF16))
    esA1.close()
    esA2 = ExitStack()

    def sbA2(name, shape, dtype):
        return esA2.enter_context(nc.sbuf_tensor(name, shape, dtype))

    sbk = [sbA2(f"sbk{p}", [128, D], BF16) for p in range(2)]
    sbv = [sbA2(f"sbv{p}", [128, D], BF16) for p in range(2)]
    sbr = [sbA2(f"sbr{p}", [128, D], BF16) for p in range(2)]
    ekb = [sbA2(f"ekb{p}", [128, TCORE], F32) for p in range(2)]
    aab = [sbA2(f"aab{p}", [128, TCORE], F32) for p in range(2)]
    nmb = [sbA2(f"nmb{p}", [128, TCORE], F32) for p in range(2)]
    dnb = [sbA2(f"dnb{p}", [128, TCORE], F32) for p in range(2)]
    rrb = [sbA2(f"rrb{p}", [128, TCORE], F32) for p in range(2)]
    esA2.close()
    esX.close()

    sxB = [sb(f"sxB{p}", [128, DT, TC], F32) for p in range(2)]
    xn2 = sb("xn2", [128, DT, TC], BF16)
    xbB = sb("xbB", [128, DT, TC], BF16)
    xb2B = sb("xb2B", [128, DT, TC], BF16)
    fuse = sb("fuse", [128, IC, TC], BF16)
    sgw = [sb(f"sgw{p}", [128, D], FP8) for p in range(2)]
    suw = [sb(f"suw{p}", [128, D], FP8) for p in range(2)]
    sdw = [sb(f"sdw{p}", [128, 48, 128], FP8) for p in range(2)]
    bnum = [sb(f"bnum{p}", [128, TC], F32) for p in range(2)]
    bden = [sb(f"bden{p}", [128, TC], F32) for p in range(2)]
    brr = [sb(f"brr{p}", [128, TC], F32) for p in range(2)]
    wpb = [sb("wpb0", [128, TC], F32)]
    tsil = [sb("tsil0", [128, TC], F32)]

    esP1 = ExitStack()
    pst = esP1.enter_context(nc.psum_tensor("pstA", [1, TC], F32))
    pbm = esP1.enter_context(nc.psum_tensor("pbmA", [128, TC], F32))
    pbr = esP1.enter_context(nc.psum_tensor("pbrA", [128, TC], F32))
    esP1.close()
    esP2 = ExitStack()
    psK = [esP2.enter_context(nc.psum_tensor(f"psK{i}", [128, TC], F32))
           for i in range(4)]
    psV = [esP2.enter_context(nc.psum_tensor(f"psV{i}", [128, TC], F32))
           for i in range(4)]
    esP2.close()
    esP3 = ExitStack()
    pst2 = esP3.enter_context(nc.psum_tensor("pstB", [1, TC], F32))
    pbm2 = esP3.enter_context(nc.psum_tensor("pbmB", [128, TC], F32))
    pbr2 = esP3.enter_context(nc.psum_tensor("pbrB", [128, TC], F32))
    psG = [esP3.enter_context(nc.psum_tensor(f"psG{i}", [128, TC], F32))
           for i in range(2)]
    psU = [esP3.enter_context(nc.psum_tensor(f"psU{i}", [128, TC], F32))
           for i in range(2)]
    psD = esP3.enter_context(nc.psum_tensor("psD", [128, TC], F32))
    esP3.close()

    Y, G, T, V, S = "sync", "gpsimd", "tensor", "vector", "scalar"

    def emit(P: Sch):
        P.do(G, lambda e: e.memset(ones_bf[:, :], 1.0), [("g", 1)])
        P.do(G, lambda e: e.memset(onesrow[:, :], 1.0), [("g", 1)])
        P.do(G, lambda e: e.memset(epsb[:, :], EPS_WKV), [("g", 1)])
        P.do(G, lambda e: e.iota(iot[:, :], pattern=[[1, TC]], base=0,
                                 channel_multiplier=0,
                                 allow_small_or_imprecise_dtypes=True), [("g", 1)])
        P.mark("consts", "g")

        for buf, src in ((s_td, td_col), (s_gs, gs_col), (s_us, us_col),
                         (s_dsc, dsc_col), (s_cm, cmask)):
            P.dma(Y, "vld", lambda e, buf=buf, src=src: e.dma_start(
                out=buf[:, :], in_=src))
        P.mark("vld_done", "vld")

        P.wait(S, "vld", P.val("vld_done"))
        P.do(S, lambda e: e.activation(out=lam[:, :], in_=s_td[:, :], func=AF.Exp),
             [("s", 1)])
        P.mark("lam_done", "s")
        P.wait(V, "s", P.val("lam_done"))
        P.do(V, lambda e: e.tensor_scalar_mul(out=neglam[:, :], in0=lam[:, :],
                                              scalar1=-1.0), [("v", 1)])
        P.mark("neglam_done", "v")
        P.wait(S, "v", P.val("neglam_done"))
        P.do(S, lambda e: e.activation(out=wcol[:, :], in_=neglam[:, :],
                                       func=AF.Exp), [("s", 1)])
        P.mark("wcol_done", "s")

        for name, sem, wsrc, sdst in (("k", "pk", wkT, scr_k),
                                      ("v", "pv", wvT, scr_v),
                                      ("r", "pr", wrT, scr_r)):
            for et in range(ET):
                src3 = wsrc[:, et * 128:(et + 1) * 128].rearrange(
                    "(j p) e -> p j e", p=128)
                for h in range(2):
                    P.dma(G, sem, lambda e, et=et, h=h, src3=src3, sdst=sdst:
                          e.dma_start(out=sdst[et][:, h * 8:(h + 1) * 8, :],
                                      in_=src3[:, h * 8:(h + 1) * 8, :]))
            P.mark(f"pro{name}", sem)

        def emit_ln(src, src_lo, n, dst, dst_lo, tag, xbt, xb2t, p_st, p_bm, p_br,
                    v_wait=(), s_wait=(), t_wait=(), rows_wait=(), apply_wait=()):
            for s_, vv in v_wait:
                P.wait(V, s_, vv)
            for j in range(DT):
                P.do(V, lambda e, j=j: e.tensor_copy(
                    out=xbt[:, j, 0:n], in_=src[:, j, src_lo:src_lo + n]),
                    [("v", 1)])
            P.mark(f"cast_{tag}", "v")
            for s_, vv in s_wait:
                P.wait(S, s_, vv)
            for j in range(DT):
                P.do(S, lambda e, j=j: e.activation(
                    out=xb2t[:, j, 0:n], in_=src[:, j, src_lo:src_lo + n],
                    func=AF.Square), [("s", 1)])
            P.mark(f"sq_{tag}", "s")
            P.wait(T, "v", P.val(f"cast_{tag}"))
            P.wait(T, "s", P.val(f"sq_{tag}"))
            P.wait(T, "g", P.val("consts"))
            for s_, vv in t_wait:
                P.wait(T, s_, vv)
            for j in range(DT):
                P.do(T, lambda e, j=j: e.matmul(
                    p_st[0:1, 0:n], ones_bf[:, :], xbt[:, j, 0:n],
                    start=(j == 0), stop=(j == DT - 1)),
                    [("t", 1)] if j == DT - 1 else [])
            for j in range(DT):
                P.do(T, lambda e, j=j: e.matmul(
                    p_bm[0:1, 0:n], ones_bf[:, :], xb2t[:, j, 0:n],
                    start=(j == 0), stop=(j == DT - 1)),
                    [("t", 1)] if j == DT - 1 else [])
            P.mark(f"stats_{tag}", "t")
            P.wait(V, "t", P.val(f"stats_{tag}"))
            for s_, vv in rows_wait:
                P.wait(V, s_, vv)
            P.do(V, lambda e: e.tensor_scalar_mul(out=r_mean[0:1, 0:n],
                                                  in0=p_st[0:1, 0:n],
                                                  scalar1=1.0 / D), [("v", 1)])
            P.do(V, lambda e: e.tensor_scalar_mul(out=r_var[0:1, 0:n],
                                                  in0=p_bm[0:1, 0:n],
                                                  scalar1=1.0 / D), [("v", 1)])
            P.do(V, lambda e: e.tensor_mul(out=r_rstd[0:1, 0:n], in0=r_mean[0:1, 0:n],
                                           in1=r_mean[0:1, 0:n]), [("v", 1)])
            P.do(V, lambda e: e.tensor_sub(out=r_var[0:1, 0:n], in0=r_var[0:1, 0:n],
                                           in1=r_rstd[0:1, 0:n]), [("v", 1)])
            P.do(V, lambda e: e.tensor_scalar_add(out=r_var[0:1, 0:n],
                                                  in0=r_var[0:1, 0:n],
                                                  scalar1=EPS_LN), [("v", 1)])
            P.do(V, lambda e: e.reciprocal(out=r_var[0:1, 0:n], in_=r_var[0:1, 0:n]),
                 [("v", 1)])
            P.mark(f"rows_{tag}", "v")
            P.wait(S, "v", P.val(f"rows_{tag}"))
            P.do(S, lambda e: e.activation(out=r_rstd[0:1, 0:n], in_=r_var[0:1, 0:n],
                                           func=AF.Sqrt), [("s", 1)])
            P.mark(f"rstd_{tag}", "s")
            P.wait(T, "s", P.val(f"rstd_{tag}"))
            P.do(T, lambda e: e.matmul(p_bm[:, 0:n], onesrow[:, :], r_mean[0:1, 0:n],
                                       start=True, stop=True), [("t", 1)])
            P.do(T, lambda e: e.matmul(p_br[:, 0:n], onesrow[:, :], r_rstd[0:1, 0:n],
                                       start=True, stop=True), [("t", 1)])
            P.mark(f"bc_{tag}", "t")
            P.wait(V, "t", P.val(f"bc_{tag}"))
            for s_, vv in apply_wait:
                P.wait(V, s_, vv)
            for j in range(DT):
                P.do(V, lambda e, j=j: e.tensor_sub(
                    out=tmpf[:, 0:n], in0=src[:, j, src_lo:src_lo + n],
                    in1=p_bm[:, 0:n]), [("v", 1)])
                P.do(V, lambda e, j=j, dst=dst: e.tensor_mul(
                    out=dst[:, j, dst_lo:dst_lo + n], in0=tmpf[:, 0:n],
                    in1=p_br[:, 0:n]), [("v", 1)])
            P.mark(f"apply_{tag}", "v")

        def emit_boundary():
            P.do(V, lambda e: e.tensor_copy(out=xbb[:, :], in_=sx[0][:, :, 0]),
                 [("v", 1)])
            P.mark("cast_bnd", "v")
            P.wait(S, "v", P.val("cast_bnd"))
            P.do(S, lambda e: e.activation(out=xbb2[:, :], in_=xbb[:, :],
                                           func=AF.Square), [("s", 1)])
            P.mark("sq_bnd", "s")
            P.wait(T, "v", P.val("cast_bnd"))
            P.wait(T, "s", P.val("sq_bnd"))
            for j in range(DT):
                P.do(T, lambda e, j=j: e.matmul(
                    pst[0:1, 0:1], ones_bf[:, :], xbb[:, j:j + 1],
                    start=(j == 0), stop=(j == DT - 1)),
                    [("t", 1)] if j == DT - 1 else [])
            for j in range(DT):
                P.do(T, lambda e, j=j: e.matmul(
                    pbm[0:1, 0:1], ones_bf[:, :], xbb2[:, j:j + 1],
                    start=(j == 0), stop=(j == DT - 1)),
                    [("t", 1)] if j == DT - 1 else [])
            P.mark("stats_bnd", "t")
            P.wait(V, "t", P.val("stats_bnd"))
            P.do(V, lambda e: e.tensor_scalar_mul(out=r_mean[0:1, 0:TC],
                                                  in0=pst[0:1, 0:TC],
                                                  scalar1=1.0 / D), [("v", 1)])
            P.do(V, lambda e: e.tensor_scalar_mul(out=r_var[0:1, 0:TC],
                                                  in0=pbm[0:1, 0:TC],
                                                  scalar1=1.0 / D), [("v", 1)])
            P.do(V, lambda e: e.tensor_mul(out=r_rstd[0:1, 0:TC],
                                           in0=r_mean[0:1, 0:TC],
                                           in1=r_mean[0:1, 0:TC]), [("v", 1)])
            P.do(V, lambda e: e.tensor_sub(out=r_var[0:1, 0:TC],
                                           in0=r_var[0:1, 0:TC],
                                           in1=r_rstd[0:1, 0:TC]), [("v", 1)])
            P.do(V, lambda e: e.tensor_scalar_add(out=r_var[0:1, 0:TC],
                                                  in0=r_var[0:1, 0:TC],
                                                  scalar1=EPS_LN), [("v", 1)])
            P.do(V, lambda e: e.reciprocal(out=r_var[0:1, 0:TC],
                                           in_=r_var[0:1, 0:TC]), [("v", 1)])
            P.mark("rows_bnd", "v")
            P.wait(S, "v", P.val("rows_bnd"))
            P.do(S, lambda e: e.activation(out=r_rstd[0:1, 0:TC],
                                           in_=r_var[0:1, 0:TC],
                                           func=AF.Sqrt), [("s", 1)])
            P.mark("rstd_bnd", "s")
            P.wait(S, "v", P.val("rows_bnd"))
            P.dma(S, "bnd", lambda e: e.dma_start(out=bnd_dram[0:1, 0:1],
                                                  in_=r_mean[0:1, 0:1]))
            P.dma(S, "bnd", lambda e: e.dma_start(out=bnd_dram[0:1, 1:2],
                                                  in_=r_rstd[0:1, 0:1]))
            P.mark("bnd_out", "bnd")
            P.wait(S, "bnd", P.val("bnd_out"))
            P.dma(S, "bnd", lambda e: e.dma_start(
                out=bndcol[:, :], in_=bnd_dram[0:1, :].partition_broadcast(128)))
            P.mark("bnd_bc", "bnd")
            P.wait(V, "bnd", P.val("bnd_bc"))
            P.do(V, lambda e: e.tensor_scalar(out=xn[:, :, 0],
                                              in0=sx[0][:, :, 0],
                                              scalar1=bndcol[:, 0:1],
                                              scalar2=bndcol[:, 1:2],
                                              op0=ALU.subtract, op1=ALU.mult),
                 [("v", 1)])
            P.mark("bnd_done_v", "v")

        for c in range(NCH):
            pa = c % 2
            if c >= 2:
                P.wait(Y, "v", max(P.val(f"apply_A{c - 2}"),
                                   P.val("bnd_done_v") if c == 2 else 0))
                P.wait(Y, "s", P.val(f"rstd_A{c - 2}"))
            for j in range(DT):
                P.dma(Y, f"sx{pa}", lambda e, j=j, c=c, pa=pa: e.dma_start(
                    out=sx[pa][:, j, 0:TC + 1],
                    in_=xin[j * 128:(j + 1) * 128, c * TC:c * TC + TC + 1]))
            P.mark(f"xA{c}", f"sx{pa}")
            emit_ln(sx[pa], 1, TC, xn, c * TC + 1, f"A{c}", xb, xb2,
                    pst, pbm, pbr,
                    v_wait=[(f"sx{pa}", P.val(f"xA{c}"))],
                    s_wait=[(f"sx{pa}", P.val(f"xA{c}"))],
                    rows_wait=([("bnd", P.val("bnd_out"))] if c == 1 else ()))
            if c == 0:
                emit_boundary()
        P.mark("xn_all", "v")
        for name, sem, wsrc, sdst in (("g", "pg", gw, scr_g),
                                      ("u", "pu", uw, scr_u)):
            for ic in range(IC):
                src3 = wsrc[:, ic * 128:(ic + 1) * 128].rearrange(
                    "(j p) e -> p j e", p=128)
                for h in range(2):
                    P.dma(G, sem, lambda e, ic=ic, h=h, src3=src3, sdst=sdst:
                          e.dma_start(out=sdst[ic][:, h * 8:(h + 1) * 8, :],
                                      in_=src3[:, h * 8:(h + 1) * 8, :]))
            P.mark(f"pro{name}", sem)
        if debug:
            P.wait(Y, "v", P.val("xn_all"))
            P.dma(Y, "dbg", lambda e: e.dma_start(out=dbg_xn, in_=xn[:, :, :]))
            P.mark("dxn", "dbg")

        for et in range(ET):
            pa = et % 2
            for pname, psem, ssem, sdst, sbuf_, mmn in (
                    ("k", "pk", f"slk{pa}", scr_k, sbk, "kmm"),
                    ("v", "pv", f"slv{pa}", scr_v, sbv, "vmm"),
                    ("r", "pr", f"slr{pa}", scr_r, sbr, "rmm")):
                P.wait(Y, psem, P.val(f"pro{pname}"))
                if et >= 2:
                    P.wait(Y, "t", P.val(f"{mmn}{et - 2}_3"))
                else:
                    P.wait(Y, "v", P.val("xn_all"))
                P.dma(Y, ssem, lambda e, et=et, pa=pa, sdst=sdst, sbuf_=sbuf_:
                      e.dma_start(out=sbuf_[pa][:, :],
                                  in_=sdst[et].rearrange("p j e -> p (j e)")))
                P.mark(f"slab_{pname}{et}", ssem)

            P.wait(T, f"slk{pa}", P.val(f"slab_k{et}"))
            if et == 0:
                P.wait(T, "v", P.val("xn_all"))
            P.wait(T, "s", P.val(f"rev{et - 1}_3") if et >= 1 else 0)
            for j in range(DT):
                for tc in range(4):
                    P.do(T, lambda e, j=j, tc=tc, pa=pa: e.matmul(
                        psK[tc][:, :], sbk[pa][:, j * 128:(j + 1) * 128],
                        xn[:, j, tc * TC:tc * TC + TC],
                        start=(j == 0), stop=(j == DT - 1)),
                        [("t", 1)] if j == DT - 1 else [])
                    if j == DT - 1:
                        P.mark(f"kmm{et}_{tc}", "t")
            P.wait(S, "v", P.val(f"dnsc{et - 2}") if et >= 2 else 0)
            for tc in range(4):
                P.wait(S, "t", P.val(f"kmm{et}_{tc}"))
                P.do(S, lambda e, tc=tc, pa=pa: e.activation(
                    out=ekb[pa][:, tc * TC:(tc + 1) * TC], in_=psK[tc][:, :],
                    func=AF.Exp), [("s", 1)])
                P.mark(f"ek{et}_{tc}", "s")
            P.wait(T, f"slv{pa}", P.val(f"slab_v{et}"))
            P.wait(T, "v", P.val(f"aa{et - 1}_3") if et >= 1 else 0)
            for j in range(DT):
                for tc in range(4):
                    P.do(T, lambda e, j=j, tc=tc, pa=pa: e.matmul(
                        psV[tc][:, :], sbv[pa][:, j * 128:(j + 1) * 128],
                        xn[:, j, tc * TC:tc * TC + TC],
                        start=(j == 0), stop=(j == DT - 1)),
                        [("t", 1)] if j == DT - 1 else [])
                    if j == DT - 1:
                        P.mark(f"vmm{et}_{tc}", "t")
            P.wait(V, f"spn{pa}", P.val(f"spn{et - 2}") if et >= 2 else 0)
            for tc in range(4):
                P.wait(V, "t", P.val(f"vmm{et}_{tc}"))
                P.wait(V, "s", P.val(f"ek{et}_{tc}"))
                P.do(V, lambda e, tc=tc, pa=pa: e.tensor_mul(
                    out=aab[pa][:, tc * TC:(tc + 1) * TC],
                    in0=ekb[pa][:, tc * TC:(tc + 1) * TC], in1=psV[tc][:, :]),
                    [("v", 1)])
                P.mark(f"aa{et}_{tc}", "v")
            P.wait(T, f"slr{pa}", P.val(f"slab_r{et}"))
            P.wait(T, "s", P.val(f"ek{et}_3"))
            for j in range(DT):
                for tc in range(4):
                    P.do(T, lambda e, j=j, tc=tc, pa=pa: e.matmul(
                        psK[tc][:, :], sbr[pa][:, j * 128:(j + 1) * 128],
                        xn[:, j, tc * TC + 1:tc * TC + TC + 1],
                        start=(j == 0), stop=(j == DT - 1)),
                        [("t", 1)] if j == DT - 1 else [])
                    if j == DT - 1:
                        P.mark(f"rmm{et}_{tc}", "t")
            if et == 0:
                P.wait(V, "s", P.val("wcol_done"))
            P.do(V, lambda e, et=et, pa=pa: e.tensor_tensor_scan(
                out=nmb[pa][:, :],
                data0=wcol[:, et:et + 1].broadcast_to([128, TCORE]),
                data1=aab[pa][:, :], initial=0.0, op0=ALU.mult, op1=ALU.add),
                [("v", 1)])
            P.mark(f"nm{et}", "v")
            P.wait(V, f"spd{pa}", P.val(f"spd{et - 2}") if et >= 2 else 0)
            P.do(V, lambda e, et=et, pa=pa: e.tensor_tensor_scan(
                out=dnb[pa][:, :],
                data0=wcol[:, et:et + 1].broadcast_to([128, TCORE]),
                data1=ekb[pa][:, :], initial=0.0, op0=ALU.mult, op1=ALU.add),
                [("v", 1)])
            P.do(V, lambda e, et=et, pa=pa: e.tensor_copy(
                out=stt[:, et:et + 1], in_=nmb[pa][:, TCORE - 1:TCORE]), [("v", 1)])
            P.mark(f"dnsc{et}", "v")
            P.wait(S, "v", P.val(f"dnsc{et}"))
            P.do(S, lambda e, pa=pa: e.activation(
                out=dnb[pa][:, :], in_=dnb[pa][:, :], func=AF.Identity,
                bias=epsb[:, 0:1]), [("s", 1)])
            P.mark(f"eps{et}", "s")
            P.wait(V, "s", P.val(f"eps{et}"))
            P.do(V, lambda e, et=et, pa=pa: e.tensor_copy(
                out=stt[:, 16 + et:17 + et], in_=dnb[pa][:, TCORE - 1:TCORE]),
                [("v", 1)])
            P.mark(f"dn{et}", "v")
            P.wait(S, f"spr{pa}", P.val(f"spr{et - 2}") if et >= 2 else 0)
            for tc in range(4):
                P.wait(S, "t", P.val(f"rmm{et}_{tc}"))
                P.do(S, lambda e, tc=tc, pa=pa: e.activation(
                    out=rrb[pa][:, tc * TC:(tc + 1) * TC], in_=psK[tc][:, :],
                    func=AF.Copy), [("s", 1)])
                if tc == 3:
                    P.mark(f"rev{et}_3", "s")
            P.wait(S, "v", P.val(f"nm{et}"))
            P.dma(S, f"spn{pa}", lambda e, et=et, pa=pa: e.dma_start(
                out=sp_num[et], in_=nmb[pa][:, :]))
            P.mark(f"spn{et}", f"spn{pa}")
            P.dma(S, f"spd{pa}", lambda e, et=et, pa=pa: e.dma_start(
                out=sp_den[et], in_=dnb[pa][:, :]))
            P.mark(f"spd{et}", f"spd{pa}")
            P.dma(S, f"spr{pa}", lambda e, et=et, pa=pa: e.dma_start(
                out=sp_r[et], in_=rrb[pa][:, :]))
            P.mark(f"spr{et}", f"spr{pa}")

        P.mark("state_all", "v")
        P.wait(Y, "v", P.val("state_all"))
        P.dma(Y, "cci", lambda e: e.dma_start(out=cc_in, in_=stt[:, :]))
        P.mark("ccin", "cci")
        P.wait(G, "cci", P.val("ccin"))
        P.do(G, lambda e: e.collective_compute(
            "AllGather", ALU.bypass,
            replica_groups=[[0, 1], [2, 3], [4, 5], [6, 7]],
            ins=[cc_in], outs=[cc_out]), [("g", 1)])
        P.mark("cc", "g")
        for hc in range(HC):
            src3 = dwn[:, hc * 128:(hc + 1) * 128].rearrange("(i p) e -> p i e", p=128)
            for h in range(6):
                P.dma(G, "pd", lambda e, hc=hc, h=h, src3=src3: e.dma_start(
                    out=scr_d[hc][:, h * 8:(h + 1) * 8, :],
                    in_=src3[:, h * 8:(h + 1) * 8, :]))
        P.mark("prod", "pd")

        P.wait(Y, "g", P.val("cc"))
        P.dma(Y, "gt", lambda e: e.dma_start(out=gath[:, :], in_=cc_out[0:128, :]))
        P.mark("gath", "gt")
        P.wait(V, "gt", P.val("gath"))
        P.do(V, lambda e: e.tensor_scalar_mul(out=ninb[:, :], in0=gath[:, 0:16],
                                              scalar1=s_cm[:, 0:1]), [("v", 1)])
        P.do(V, lambda e: e.tensor_scalar_mul(out=dinb[:, :], in0=gath[:, 16:32],
                                              scalar1=s_cm[:, 0:1]), [("v", 1)])
        P.mark("nin_done", "v")
        if debug:
            P.wait(Y, "gt", P.val("gath"))
            P.dma(Y, "dbg", lambda e: e.dma_start(out=dbg_gath, in_=gath[:, :]))

        def xB_load(c):
            if c == 0:
                for p in range(2):
                    P.wait(Y, f"spn{p}", P.val(f"spn{14 + p}"))
                    P.wait(Y, f"spd{p}", P.val(f"spd{14 + p}"))
                    P.wait(Y, f"spr{p}", P.val(f"spr{14 + p}"))
                if debug:
                    P.wait(Y, "dbg", P.val("dxn"))
            if c >= 2:
                P.wait(Y, f"ou{c % 2}", P.val(f"out{c - 2}"))
            for j in range(DT):
                P.dma(Y, f"xb{c % 2}", lambda e, j=j, c=c: e.dma_start(
                    out=sxB[c % 2][:, j, :],
                    in_=xin[j * 128:(j + 1) * 128, 1 + c * TC:1 + (c + 1) * TC]))
            P.mark(f"xB{c}", f"xb{c % 2}")

        def combine_head(c):
            P.do(V, lambda e, c=c: e.tensor_scalar_mul(
                out=nlb[:, :], in0=neglam[:, :], scalar1=float(c * TC + 1)),
                [("v", 1)])
            P.mark(f"nlb{c}", "v")

        def combine_et(c, et):
            pa = et % 2
            prev = (P.val(f"comb{c}_{et - 2}") if et >= 2 else
                    (P.val(f"comb{c - 1}_{14 + et}") if c >= 1 else 0))
            P.wait(G, "v", prev)
            for src, dstb, bsem, spsem in (
                    (sp_num, bnum, f"bn{pa}", "spn"),
                    (sp_den, bden, f"bd{pa}", "spd"),
                    (sp_r, brr, f"br{pa}", "spr")):
                P.wait(G, f"{spsem}{pa}", P.val(f"{spsem}{et}"))
                P.dma(G, bsem, lambda e, et=et, src=src, dstb=dstb, c=c, pa=pa:
                      e.dma_start(out=dstb[pa][:, :],
                                  in_=src[et][:, c * TC:(c + 1) * TC]))
                P.mark(f"{bsem}m{c}_{et}", bsem)
            prev1 = (P.val(f"comb{c}_{et - 1}") if et >= 1 else
                     (P.val(f"comb{c - 1}_15") if c >= 1 else 0))
            P.wait(S, "v", max(prev1, P.val(f"nlb{c}")))
            P.do(S, lambda e, et=et: e.activation(
                out=wpb[0][:, :], in_=iot[:, :], func=AF.Exp,
                scale=neglam[:, et:et + 1], bias=nlb[:, et:et + 1]),
                [("s", 1)])
            P.mark(f"wp{c}_{et}", "s")
            P.wait(V, f"bn{pa}", P.val(f"bn{pa}m{c}_{et}"))
            P.wait(V, f"bd{pa}", P.val(f"bd{pa}m{c}_{et}"))
            P.wait(V, f"br{pa}", P.val(f"br{pa}m{c}_{et}"))
            P.wait(V, "s", P.val(f"wp{c}_{et}"))
            if et == 0:
                P.wait(V, f"xb{c % 2}", P.val(f"xB{c}"))
            P.do(V, lambda e, et=et, pa=pa: e.scalar_tensor_tensor(
                out=bnum[pa][:, :], in0=wpb[0][:, :],
                scalar=ninb[:, et:et + 1], in1=bnum[pa][:, :],
                op0=ALU.mult, op1=ALU.add), [("v", 1)])
            P.do(V, lambda e, et=et, pa=pa: e.scalar_tensor_tensor(
                out=bden[pa][:, :], in0=wpb[0][:, :],
                scalar=dinb[:, et:et + 1], in1=bden[pa][:, :],
                op0=ALU.mult, op1=ALU.add), [("v", 1)])
            P.do(V, lambda e, pa=pa: e.reciprocal(out=wpb[0][:, :],
                                                  in_=bden[pa][:, :]),
                 [("v", 1)])
            P.do(V, lambda e, pa=pa: e.tensor_mul(
                out=bnum[pa][:, :], in0=bnum[pa][:, :], in1=wpb[0][:, :]),
                [("v", 1)])
            P.do(V, lambda e, pa=pa: e.tensor_mul(
                out=brr[pa][:, :], in0=brr[pa][:, :], in1=bnum[pa][:, :]),
                [("v", 1)])
            P.do(V, lambda e, et=et, pa=pa, c=c: e.tensor_add(
                out=sxB[c % 2][:, et, :], in0=sxB[c % 2][:, et, :],
                in1=brr[pa][:, :]), [("v", 1)])
            P.mark(f"comb{c}_{et}", "v")

        def combine_tail(c):
            if debug:
                P.wait(Y, "v", P.val(f"comb{c}_15"))
                P.dma(Y, "dbg", lambda e, c=c: e.dma_start(
                    out=dbg_x1[c], in_=sxB[c % 2][:, :, :]))
                P.mark(f"dx1_{c}", "dbg")

        def combine(c):
            combine_head(c)
            for et in range(ET):
                combine_et(c, et)
            combine_tail(c)

        def ln2(c):
            emit_ln(sxB[c % 2], 0, TC, xn2, 0, f"B{c}", xbB, xb2B,
                    pst2, pbm2, pbr2,
                    s_wait=[("v", P.val(f"comb{c}_15"))],
                    t_wait=([("s", P.val("rev15_3"))] if c == 0 else ()),
                    apply_wait=([("dbg", P.val(f"dxn2_{c - 1}"))]
                                if (debug and c >= 1) else ()))
            if debug:
                P.wait(Y, "v", P.val(f"apply_B{c}"))
                P.dma(Y, "dbg", lambda e, c=c: e.dma_start(
                    out=dbg_xn2[c], in_=xn2[:, :, :]))
                P.mark(f"dxn2_{c}", "dbg")

        def gate_up(c):
            for ic in range(IC):
                pb = ic % 2
                for wsem, ssem, sdst, sbuf_, mmn in (
                        ("pg", f"gs{pb}", scr_g, sgw, "gmm"),
                        ("pu", f"us{pb}", scr_u, suw, "umm")):
                    P.wait(Y, wsem, P.val(f"pro{wsem[1]}"))
                    P.wait(Y, "t", P.val(f"{mmn}{c}_{ic - 2}") if ic >= 2 else
                           (P.val(f"{mmn}{c - 1}_{IC - 2 + ic}") if c >= 1 else 0))
                    P.dma(Y, ssem, lambda e, ic=ic, pb=pb, sdst=sdst, sbuf_=sbuf_:
                          e.dma_start(out=sbuf_[pb][:, :],
                                      in_=sdst[ic].rearrange("p j e -> p (j e)")))
                    P.mark(f"{mmn}sl{c}_{ic}", ssem)
                P.wait(T, f"gs{pb}", P.val(f"gmmsl{c}_{ic}"))
                if ic == 0:
                    if c == 0:
                        P.wait(T, "s", P.val("rev15_3"))
                    P.wait(T, "v", P.val(f"apply_B{c}"))
                P.wait(T, "v", P.val(f"fuse{c}_{ic - 2}") if ic >= 2 else
                       (P.val(f"fuse{c - 1}_{IC - 2 + ic}") if c >= 1 else 0))
                for j in range(DT):
                    P.do(T, lambda e, j=j, ic=ic, pb=pb: e.matmul(
                        psG[pb][:, :], sgw[pb][:, j * 128:(j + 1) * 128],
                        xn2[:, j, :], start=(j == 0), stop=(j == DT - 1)),
                        [("t", 1)] if j == DT - 1 else [])
                P.mark(f"gmm{c}_{ic}", "t")
                P.wait(T, f"us{pb}", P.val(f"ummsl{c}_{ic}"))
                for j in range(DT):
                    P.do(T, lambda e, j=j, ic=ic, pb=pb: e.matmul(
                        psU[pb][:, :], suw[pb][:, j * 128:(j + 1) * 128],
                        xn2[:, j, :], start=(j == 0), stop=(j == DT - 1)),
                        [("t", 1)] if j == DT - 1 else [])
                P.mark(f"umm{c}_{ic}", "t")
                P.wait(S, "t", P.val(f"gmm{c}_{ic}"))
                P.wait(S, "v", P.val(f"fuse{c}_{ic - 1}") if ic >= 1 else
                       (P.val(f"fuse{c - 1}_{IC - 1}") if c >= 1 else 0))
                P.do(S, lambda e, ic=ic, pb=pb: e.activation(
                    out=tsil[0][:, :], in_=psG[pb][:, :], func=AF.Silu,
                    scale=s_gs[:, ic:ic + 1]), [("s", 1)])
                P.mark(f"silu{c}_{ic}", "s")
                P.wait(V, "t", P.val(f"umm{c}_{ic}"))
                P.wait(V, "s", P.val(f"silu{c}_{ic}"))
                P.do(V, lambda e, ic=ic, pb=pb: e.scalar_tensor_tensor(
                    out=fuse[:, ic, :], in0=psU[pb][:, :],
                    scalar=s_us[:, ic:ic + 1], in1=tsil[0][:, :],
                    op0=ALU.mult, op1=ALU.mult), [("v", 1)])
                P.mark(f"fuse{c}_{ic}", "v")

        def down(c, nxt=None):
            if nxt is not None:
                combine_head(nxt)
            for hc in range(HC):
                pb = hc % 2
                P.wait(Y, "pd", P.val("prod"))
                P.wait(Y, "t", P.val(f"dmm{c}_{hc - 2}") if hc >= 2 else
                       (P.val(f"dmm{c - 1}_{HC - 2 + hc}") if c >= 1 else 0))
                P.dma(Y, f"ds{pb}", lambda e, hc=hc, pb=pb: e.dma_start(
                    out=sdw[pb][:, :, :], in_=scr_d[hc]))
                P.mark(f"dsl{c}_{hc}", f"ds{pb}")
                P.wait(T, f"ds{pb}", P.val(f"dsl{c}_{hc}"))
                if hc == 0:
                    P.wait(T, "v", P.val(f"fuse{c}_{IC - 1}"))
                P.wait(T, "v", P.val(f"dev{c}_{hc - 1}") if hc >= 1 else
                       (P.val(f"dev{c - 1}_{HC - 1}") if c >= 1 else 0))
                for i in range(48):
                    P.do(T, lambda e, i=i, hc=hc, pb=pb: e.matmul(
                        psD[:, :], sdw[pb][:, i, :], fuse[:, i, :],
                        start=(i == 0), stop=(i == 47)),
                        [("t", 1)] if i == 47 else [])
                P.mark(f"dmm{c}_{hc}", "t")
                P.wait(V, "t", P.val(f"dmm{c}_{hc}"))
                if debug and hc == 0:
                    P.wait(V, "dbg", P.val(f"dx1_{c}"))
                P.do(V, lambda e, hc=hc, c=c: e.scalar_tensor_tensor(
                    out=sxB[c % 2][:, hc, :], in0=psD[:, :],
                    scalar=s_dsc[:, hc:hc + 1], in1=sxB[c % 2][:, hc, :],
                    op0=ALU.mult, op1=ALU.add), [("v", 1)])
                P.mark(f"dev{c}_{hc}", "v")
                if nxt is not None:
                    combine_et(nxt, hc)
            if nxt is not None:
                combine_tail(nxt)

        def out_store(c):
            P.wait(Y, "v", P.val(f"dev{c}_{HC - 1}"))
            for j in range(DT):
                P.dma(Y, f"ou{c % 2}", lambda e, j=j, c=c: e.dma_start(
                    out=xout[j * 128:(j + 1) * 128, c * TC:(c + 1) * TC],
                    in_=sxB[c % 2][:, j, :]))
            P.mark(f"out{c}", f"ou{c % 2}")

        xB_load(0)
        combine(0)
        ln2(0)
        for c in range(NCH):
            gate_up(c)
            if c + 1 < NCH:
                xB_load(c + 1)
            down(c, nxt=(c + 1 if c + 1 < NCH else None))
            if c + 1 < NCH:
                ln2(c + 1)
            out_store(c)

        P.wait(Y, "ou0", P.val("out2"))
        P.wait(Y, "ou1", P.val("out3"))

    sems = {}
    sem_es = ExitStack()
    for s in SEMS:
        sems[s] = sem_es.enter_context(nc.semaphore(s))

    marks = {}
    emit(Sch(nc, None, sems, marks))

    with nc.Block() as block:
        @block.sync
        def _(sync):
            emit(Sch(nc, "sync", sems, marks))

        @block.gpsimd
        def _(gpsimd):
            emit(Sch(nc, "gpsimd", sems, marks))

        @block.tensor
        def _(tensor):
            emit(Sch(nc, "tensor", sems, marks))

        @block.vector
        def _(vector):
            emit(Sch(nc, "vector", sems, marks))

        @block.scalar
        def _(scalar):
            emit(Sch(nc, "scalar", sems, marks))

    sem_es.close()
    es.close()
    return nc


_NC_CACHE = None


def kernel(L0_attn_x, L0_attn_Wk, L0_attn_Wv, L0_attn_Wr, L0_attn_time_decay,
           L0_ffn_gate_w, L0_ffn_up_w, L0_ffn_down_w, L0_ffn_gate_scale,
           L0_ffn_up_scale, L0_ffn_down_scale):
    global _NC_CACHE
    x = np.asarray(L0_attn_x, dtype=np.float32)
    B, T, Dm = x.shape
    assert (B, T, Dm) == (4, 4096, 2048)

    wkTh = np.ascontiguousarray(np.asarray(L0_attn_Wk, np.float32).T)
    wvTh = np.ascontiguousarray(np.asarray(L0_attn_Wv, np.float32).T)
    wrTh = np.ascontiguousarray(np.asarray(L0_attn_Wr, np.float32).T)
    gwh = np.ascontiguousarray(np.asarray(L0_ffn_gate_w, np.float32))
    uwh = np.ascontiguousarray(np.asarray(L0_ffn_up_w, np.float32))
    dwh = np.ascontiguousarray(np.asarray(L0_ffn_down_w, np.float32))
    td_colh = np.ascontiguousarray(
        np.asarray(L0_attn_time_decay, np.float32).reshape(16, 128).T)
    gs_colh = np.ascontiguousarray(
        np.asarray(L0_ffn_gate_scale, np.float32).reshape(48, 128).T)
    us_colh = np.ascontiguousarray(
        np.asarray(L0_ffn_up_scale, np.float32).reshape(48, 128).T)
    ds_colh = np.ascontiguousarray(
        np.asarray(L0_ffn_down_scale, np.float32).reshape(16, 128).T)

    in_maps = []
    for c in range(8):
        b, h = c // 2, c % 2
        t0 = h * TCORE
        xinh = np.empty((Dm, TCORE + 1), np.float32)
        if h == 0:
            xinh[:, 0] = 0.0
        else:
            xinh[:, 0] = x[b, t0 - 1]
        xinh[:, 1:] = x[b, t0:t0 + TCORE].T
        cm = np.full((128, 1), 1.0 if h == 1 else 0.0, np.float32)
        in_maps.append(dict(xin=np.ascontiguousarray(xinh), wkT=wkTh, wvT=wvTh,
                            wrT=wrTh, gw=gwh, uw=uwh, dwn=dwh, td_col=td_colh,
                            gs_col=gs_colh, us_col=us_colh, dsc_col=ds_colh,
                            cmask=cm))

    if _NC_CACHE is None:
        _NC_CACHE = build_program()
    res = run_bass_kernel_spmd(_NC_CACHE, in_maps, core_ids=list(range(8)))

    out = np.empty((B, T, Dm), np.float32)
    for c in range(8):
        b, h = c // 2, c % 2
        t0 = h * TCORE
        out[b, t0:t0 + TCORE] = res.results[c]["xout"].T
    return out


# version 06
# speedup vs baseline: 1.0920x; 1.0920x over previous
import numpy as np
from contextlib import ExitStack

import concourse.bass as bass
import concourse.mybir as mybir
from concourse.bass_utils import run_bass_kernel_spmd

F32 = mybir.dt.float32
BF16 = mybir.dt.bfloat16
FP8 = mybir.dt.float8e4
AF = mybir.ActivationFunctionType
ALU = mybir.AluOpType

D = 2048
DT = 16
TCORE = 2048
TC = 512
NCH = 4
ET = 16
IC = 48
HC = 16
EPS_LN = 1e-5
EPS_WKV = 1e-8

SEMS = (["t", "v", "s", "g", "vld", "pk", "pv", "pr", "pg", "pu", "pd",
         "gt", "cci", "bnd", "dbg"]
        + [f"{n}{p}" for n in ("sx", "slk", "slv", "slr", "spn", "spd", "spr",
                               "xb", "gs", "us", "ds", "bn", "bd", "br", "ou")
           for p in range(2)])


class Sch:

    def __init__(self, nc, eng, sems, marks):
        self.nc = nc
        self.which = eng
        self.sems = sems
        self.M = marks
        self.dry = eng is None
        self.cnt = {k: 0 for k in sems}

    def do(self, eng, fn, incs=()):
        inst = None
        if self.which == eng:
            inst = fn(getattr(self.nc, eng))
        for s, k in incs:
            self.cnt[s] += k
            if inst is not None:
                inst.then_inc(self.sems[s], k)
        return inst

    def dma(self, eng, sem, fn):
        return self.do(eng, fn, [(sem, 16)])

    def wait(self, eng, s, val):
        if val and self.which == eng:
            getattr(self.nc, eng).wait_ge(self.sems[s], val)

    def mark(self, name, s):
        v = self.cnt[s]
        if self.dry:
            self.M[name] = v
        else:
            assert self.M[name] == v, (name, self.M[name], v)

    def val(self, name):
        return self.M.get(name, 0)


def build_program(debug=False):
    nc = bass.Bass("TRN2", target_bir_lowering=False)

    xin = nc.dram_tensor("xin", [D, TCORE + 1], F32, kind="ExternalInput").ap()
    wkT = nc.dram_tensor("wkT", [D, D], F32, kind="ExternalInput").ap()
    wvT = nc.dram_tensor("wvT", [D, D], F32, kind="ExternalInput").ap()
    wrT = nc.dram_tensor("wrT", [D, D], F32, kind="ExternalInput").ap()
    gw = nc.dram_tensor("gw", [D, 6144], F32, kind="ExternalInput").ap()
    uw = nc.dram_tensor("uw", [D, 6144], F32, kind="ExternalInput").ap()
    dwn = nc.dram_tensor("dwn", [6144, D], F32, kind="ExternalInput").ap()
    td_col = nc.dram_tensor("td_col", [128, 16], F32, kind="ExternalInput").ap()
    gs_col = nc.dram_tensor("gs_col", [128, 48], F32, kind="ExternalInput").ap()
    us_col = nc.dram_tensor("us_col", [128, 48], F32, kind="ExternalInput").ap()
    dsc_col = nc.dram_tensor("dsc_col", [128, 16], F32, kind="ExternalInput").ap()
    cmask = nc.dram_tensor("cmask", [128, 1], F32, kind="ExternalInput").ap()

    xout = nc.dram_tensor("xout", [D, TCORE], F32, kind="ExternalOutput").ap()

    scr_k = nc.dram_tensor("scr_k", [ET, 128, DT, 128], BF16, kind="Internal").ap()
    scr_v = nc.dram_tensor("scr_v", [ET, 128, DT, 128], BF16, kind="Internal").ap()
    scr_r = nc.dram_tensor("scr_r", [ET, 128, DT, 128], BF16, kind="Internal").ap()
    scr_g = nc.dram_tensor("scr_g", [IC, 128, DT, 128], FP8, kind="Internal").ap()
    scr_u = nc.dram_tensor("scr_u", [IC, 128, DT, 128], FP8, kind="Internal").ap()
    scr_d = nc.dram_tensor("scr_d", [HC, 128, 48, 128], FP8, kind="Internal").ap()
    spk = "ExternalOutput" if debug else "Internal"
    sp_num = nc.dram_tensor("sp_num", [ET, 128, TCORE], F32, kind=spk).ap()
    sp_den = nc.dram_tensor("sp_den", [ET, 128, TCORE], F32, kind=spk).ap()
    sp_r = nc.dram_tensor("sp_r", [ET, 128, TCORE], F32, kind=spk).ap()
    cc_in = nc.dram_tensor("cc_in", [128, 32], F32, kind="Internal").ap()
    cc_out = nc.dram_tensor("cc_out", [256, 32], F32, kind="Internal").ap()
    bnd_dram = nc.dram_tensor("bnd_dram", [1, 2], F32, kind="Internal").ap()
    if debug:
        dbg_xn = nc.dram_tensor("dbg_xn", [128, DT, TCORE + 4], BF16,
                                kind="ExternalOutput").ap()
        dbg_gath = nc.dram_tensor("dbg_gath", [128, 32], F32,
                                  kind="ExternalOutput").ap()
        dbg_x1 = nc.dram_tensor("dbg_x1", [NCH, 128, DT, TC], F32,
                                kind="ExternalOutput").ap()
        dbg_xn2 = nc.dram_tensor("dbg_xn2", [NCH, 128, DT, TC], BF16,
                                 kind="ExternalOutput").ap()

    es = ExitStack()

    def sb(name, shape, dtype):
        return es.enter_context(nc.sbuf_tensor(name, shape, dtype))

    ones_bf = sb("ones_bf", [128, 1], BF16)
    onesrow = sb("onesrow", [1, 128], F32)
    iot = sb("iot", [128, TC], F32)
    s_td = sb("s_td", [128, 16], F32)
    lam = sb("lam", [128, 16], F32)
    neglam = sb("neglam", [128, 16], F32)
    wcol = sb("wcol", [128, 16], F32)
    nlb = sb("nlb", [128, 16], F32)
    s_gs = sb("s_gs", [128, 48], F32)
    s_us = sb("s_us", [128, 48], F32)
    s_dsc = sb("s_dsc", [128, 16], F32)
    s_cm = sb("s_cm", [128, 1], F32)
    epsb = sb("epsb", [128, 1], F32)
    bndcol = sb("bndcol", [128, 2], F32)
    xbb = sb("xbb", [128, 16], BF16)
    xbb2 = sb("xbb2", [128, 16], BF16)
    stt = sb("stt", [128, 32], F32)
    gath = sb("gath", [128, 32], F32)
    ninb = sb("ninb", [128, 16], F32)
    dinb = sb("dinb", [128, 16], F32)
    r_mean = sb("r_mean", [1, TC], F32)
    r_var = sb("r_var", [1, TC], F32)
    r_rstd = sb("r_rstd", [1, TC], F32)
    tmpf = sb("tmpf", [128, TC], F32)

    esX = ExitStack()
    xn = esX.enter_context(nc.sbuf_tensor("xn", [128, DT, TCORE + 4], BF16))
    esA1 = ExitStack()
    sx = [esA1.enter_context(nc.sbuf_tensor(f"sx{p}", [128, DT, TC + 1], F32))
          for p in range(2)]
    xb = esA1.enter_context(nc.sbuf_tensor("xb", [128, DT, TC], BF16))
    xb2 = esA1.enter_context(nc.sbuf_tensor("xb2", [128, DT, TC], BF16))
    esA1.close()
    esA2 = ExitStack()

    def sbA2(name, shape, dtype):
        return esA2.enter_context(nc.sbuf_tensor(name, shape, dtype))

    sbk = [sbA2(f"sbk{p}", [128, D], BF16) for p in range(2)]
    sbv = [sbA2(f"sbv{p}", [128, D], BF16) for p in range(2)]
    sbr = [sbA2(f"sbr{p}", [128, D], BF16) for p in range(2)]
    ekb = [sbA2(f"ekb{p}", [128, TCORE], F32) for p in range(2)]
    aab = [sbA2(f"aab{p}", [128, TCORE], F32) for p in range(2)]
    nmb = [sbA2(f"nmb{p}", [128, TCORE], F32) for p in range(2)]
    dnb = [sbA2(f"dnb{p}", [128, TCORE], F32) for p in range(2)]
    rrb = [sbA2(f"rrb{p}", [128, TCORE], F32) for p in range(2)]
    esA2.close()
    esX.close()

    sxB = [sb(f"sxB{p}", [128, DT, TC], F32) for p in range(2)]
    xn2 = sb("xn2", [128, DT, TC], BF16)
    xbB = sb("xbB", [128, DT, TC], BF16)
    xb2B = sb("xb2B", [128, DT, TC], BF16)
    fuse = sb("fuse", [128, IC, TC], BF16)
    sgw = [sb(f"sgw{p}", [128, D], FP8) for p in range(2)]
    suw = [sb(f"suw{p}", [128, D], FP8) for p in range(2)]
    sdw = [sb(f"sdw{p}", [128, 48, 128], FP8) for p in range(2)]
    bnum = [sb(f"bnum{p}", [128, TC], F32) for p in range(2)]
    bden = [sb(f"bden{p}", [128, TC], F32) for p in range(2)]
    brr = [sb(f"brr{p}", [128, TC], F32) for p in range(2)]
    wpb = [sb("wpb0", [128, TC], F32)]
    tsil = [sb("tsil0", [128, TC], F32)]

    esP1 = ExitStack()
    pst = esP1.enter_context(nc.psum_tensor("pstA", [1, TC], F32))
    pbm = esP1.enter_context(nc.psum_tensor("pbmA", [128, TC], F32))
    pbr = esP1.enter_context(nc.psum_tensor("pbrA", [128, TC], F32))
    esP1.close()
    esP2 = ExitStack()
    psK = [esP2.enter_context(nc.psum_tensor(f"psK{i}", [128, TC], F32))
           for i in range(4)]
    psV = [esP2.enter_context(nc.psum_tensor(f"psV{i}", [128, TC], F32))
           for i in range(4)]
    esP2.close()
    esP3 = ExitStack()
    pst2 = esP3.enter_context(nc.psum_tensor("pstB", [1, TC], F32))
    pbm2 = esP3.enter_context(nc.psum_tensor("pbmB", [128, TC], F32))
    pbr2 = esP3.enter_context(nc.psum_tensor("pbrB", [128, TC], F32))
    psG = [esP3.enter_context(nc.psum_tensor(f"psG{i}", [128, TC], F32))
           for i in range(2)]
    psU = [esP3.enter_context(nc.psum_tensor(f"psU{i}", [128, TC], F32))
           for i in range(2)]
    psD = esP3.enter_context(nc.psum_tensor("psD", [128, TC], F32))
    esP3.close()

    Y, G, T, V, S = "sync", "gpsimd", "tensor", "vector", "scalar"

    def emit(P: Sch):
        P.do(G, lambda e: e.memset(ones_bf[:, :], 1.0), [("g", 1)])
        P.do(G, lambda e: e.memset(onesrow[:, :], 1.0), [("g", 1)])
        P.do(G, lambda e: e.memset(epsb[:, :], EPS_WKV), [("g", 1)])
        P.do(G, lambda e: e.iota(iot[:, :], pattern=[[1, TC]], base=0,
                                 channel_multiplier=0,
                                 allow_small_or_imprecise_dtypes=True), [("g", 1)])
        P.mark("consts", "g")

        for buf, src in ((s_td, td_col), (s_gs, gs_col), (s_us, us_col),
                         (s_dsc, dsc_col), (s_cm, cmask)):
            P.dma(Y, "vld", lambda e, buf=buf, src=src: e.dma_start(
                out=buf[:, :], in_=src))
        P.mark("vld_done", "vld")

        P.wait(S, "vld", P.val("vld_done"))
        P.do(S, lambda e: e.activation(out=lam[:, :], in_=s_td[:, :], func=AF.Exp),
             [("s", 1)])
        P.mark("lam_done", "s")
        P.wait(V, "s", P.val("lam_done"))
        P.do(V, lambda e: e.tensor_scalar_mul(out=neglam[:, :], in0=lam[:, :],
                                              scalar1=-1.0), [("v", 1)])
        P.mark("neglam_done", "v")
        P.wait(S, "v", P.val("neglam_done"))
        P.do(S, lambda e: e.activation(out=wcol[:, :], in_=neglam[:, :],
                                       func=AF.Exp), [("s", 1)])
        P.mark("wcol_done", "s")

        for name, sem, wsrc, sdst in (("k", "pk", wkT, scr_k),
                                      ("v", "pv", wvT, scr_v),
                                      ("r", "pr", wrT, scr_r)):
            for et in range(ET):
                src3 = wsrc[:, et * 128:(et + 1) * 128].rearrange(
                    "(j p) e -> p j e", p=128)
                for h in range(2):
                    P.dma(G, sem, lambda e, et=et, h=h, src3=src3, sdst=sdst:
                          e.dma_start(out=sdst[et][:, h * 8:(h + 1) * 8, :],
                                      in_=src3[:, h * 8:(h + 1) * 8, :]))
            P.mark(f"pro{name}", sem)

        def emit_ln(src, src_lo, n, dst, dst_lo, tag, xbt, xb2t, p_st, p_bm, p_br,
                    v_wait=(), s_wait=(), t_wait=(), rows_wait=(), apply_wait=()):
            for s_, vv in v_wait:
                P.wait(V, s_, vv)
            for j in range(DT):
                P.do(V, lambda e, j=j: e.tensor_copy(
                    out=xbt[:, j, 0:n], in_=src[:, j, src_lo:src_lo + n]),
                    [("v", 1)])
            P.mark(f"cast_{tag}", "v")
            for s_, vv in s_wait:
                P.wait(S, s_, vv)
            for j in range(DT):
                P.do(S, lambda e, j=j: e.activation(
                    out=xb2t[:, j, 0:n], in_=src[:, j, src_lo:src_lo + n],
                    func=AF.Square), [("s", 1)])
            P.mark(f"sq_{tag}", "s")
            P.wait(T, "v", P.val(f"cast_{tag}"))
            P.wait(T, "s", P.val(f"sq_{tag}"))
            P.wait(T, "g", P.val("consts"))
            for s_, vv in t_wait:
                P.wait(T, s_, vv)
            for j in range(DT):
                P.do(T, lambda e, j=j: e.matmul(
                    p_st[0:1, 0:n], ones_bf[:, :], xbt[:, j, 0:n],
                    start=(j == 0), stop=(j == DT - 1)),
                    [("t", 1)] if j == DT - 1 else [])
            for j in range(DT):
                P.do(T, lambda e, j=j: e.matmul(
                    p_bm[0:1, 0:n], ones_bf[:, :], xb2t[:, j, 0:n],
                    start=(j == 0), stop=(j == DT - 1)),
                    [("t", 1)] if j == DT - 1 else [])
            P.mark(f"stats_{tag}", "t")
            P.wait(V, "t", P.val(f"stats_{tag}"))
            for s_, vv in rows_wait:
                P.wait(V, s_, vv)
            P.do(V, lambda e: e.tensor_scalar_mul(out=r_mean[0:1, 0:n],
                                                  in0=p_st[0:1, 0:n],
                                                  scalar1=1.0 / D), [("v", 1)])
            P.do(V, lambda e: e.tensor_scalar_mul(out=r_var[0:1, 0:n],
                                                  in0=p_bm[0:1, 0:n],
                                                  scalar1=1.0 / D), [("v", 1)])
            P.do(V, lambda e: e.tensor_mul(out=r_rstd[0:1, 0:n], in0=r_mean[0:1, 0:n],
                                           in1=r_mean[0:1, 0:n]), [("v", 1)])
            P.do(V, lambda e: e.tensor_sub(out=r_var[0:1, 0:n], in0=r_var[0:1, 0:n],
                                           in1=r_rstd[0:1, 0:n]), [("v", 1)])
            P.do(V, lambda e: e.tensor_scalar_add(out=r_var[0:1, 0:n],
                                                  in0=r_var[0:1, 0:n],
                                                  scalar1=EPS_LN), [("v", 1)])
            P.do(V, lambda e: e.reciprocal(out=r_var[0:1, 0:n], in_=r_var[0:1, 0:n]),
                 [("v", 1)])
            P.mark(f"rows_{tag}", "v")
            P.wait(S, "v", P.val(f"rows_{tag}"))
            P.do(S, lambda e: e.activation(out=r_rstd[0:1, 0:n], in_=r_var[0:1, 0:n],
                                           func=AF.Sqrt), [("s", 1)])
            P.mark(f"rstd_{tag}", "s")
            P.wait(T, "s", P.val(f"rstd_{tag}"))
            P.do(T, lambda e: e.matmul(p_bm[:, 0:n], onesrow[:, :], r_mean[0:1, 0:n],
                                       start=True, stop=True), [("t", 1)])
            P.do(T, lambda e: e.matmul(p_br[:, 0:n], onesrow[:, :], r_rstd[0:1, 0:n],
                                       start=True, stop=True), [("t", 1)])
            P.mark(f"bc_{tag}", "t")
            P.wait(V, "t", P.val(f"bc_{tag}"))
            for s_, vv in apply_wait:
                P.wait(V, s_, vv)
            for j in range(DT):
                P.do(V, lambda e, j=j: e.tensor_sub(
                    out=tmpf[:, 0:n], in0=src[:, j, src_lo:src_lo + n],
                    in1=p_bm[:, 0:n]), [("v", 1)])
                P.do(V, lambda e, j=j, dst=dst: e.tensor_mul(
                    out=dst[:, j, dst_lo:dst_lo + n], in0=tmpf[:, 0:n],
                    in1=p_br[:, 0:n]), [("v", 1)])
                P.mark(f"apply_{tag}_j{j}", "v")
            P.mark(f"apply_{tag}", "v")

        def emit_boundary():
            P.do(V, lambda e: e.tensor_copy(out=xbb[:, :], in_=sx[0][:, :, 0]),
                 [("v", 1)])
            P.mark("cast_bnd", "v")
            P.wait(S, "v", P.val("cast_bnd"))
            P.do(S, lambda e: e.activation(out=xbb2[:, :], in_=xbb[:, :],
                                           func=AF.Square), [("s", 1)])
            P.mark("sq_bnd", "s")
            P.wait(T, "v", P.val("cast_bnd"))
            P.wait(T, "s", P.val("sq_bnd"))
            for j in range(DT):
                P.do(T, lambda e, j=j: e.matmul(
                    pst[0:1, 0:1], ones_bf[:, :], xbb[:, j:j + 1],
                    start=(j == 0), stop=(j == DT - 1)),
                    [("t", 1)] if j == DT - 1 else [])
            for j in range(DT):
                P.do(T, lambda e, j=j: e.matmul(
                    pbm[0:1, 0:1], ones_bf[:, :], xbb2[:, j:j + 1],
                    start=(j == 0), stop=(j == DT - 1)),
                    [("t", 1)] if j == DT - 1 else [])
            P.mark("stats_bnd", "t")
            P.wait(V, "t", P.val("stats_bnd"))
            P.do(V, lambda e: e.tensor_scalar_mul(out=r_mean[0:1, 0:TC],
                                                  in0=pst[0:1, 0:TC],
                                                  scalar1=1.0 / D), [("v", 1)])
            P.do(V, lambda e: e.tensor_scalar_mul(out=r_var[0:1, 0:TC],
                                                  in0=pbm[0:1, 0:TC],
                                                  scalar1=1.0 / D), [("v", 1)])
            P.do(V, lambda e: e.tensor_mul(out=r_rstd[0:1, 0:TC],
                                           in0=r_mean[0:1, 0:TC],
                                           in1=r_mean[0:1, 0:TC]), [("v", 1)])
            P.do(V, lambda e: e.tensor_sub(out=r_var[0:1, 0:TC],
                                           in0=r_var[0:1, 0:TC],
                                           in1=r_rstd[0:1, 0:TC]), [("v", 1)])
            P.do(V, lambda e: e.tensor_scalar_add(out=r_var[0:1, 0:TC],
                                                  in0=r_var[0:1, 0:TC],
                                                  scalar1=EPS_LN), [("v", 1)])
            P.do(V, lambda e: e.reciprocal(out=r_var[0:1, 0:TC],
                                           in_=r_var[0:1, 0:TC]), [("v", 1)])
            P.mark("rows_bnd", "v")
            P.wait(S, "v", P.val("rows_bnd"))
            P.do(S, lambda e: e.activation(out=r_rstd[0:1, 0:TC],
                                           in_=r_var[0:1, 0:TC],
                                           func=AF.Sqrt), [("s", 1)])
            P.mark("rstd_bnd", "s")
            P.wait(S, "v", P.val("rows_bnd"))
            P.dma(S, "bnd", lambda e: e.dma_start(out=bnd_dram[0:1, 0:1],
                                                  in_=r_mean[0:1, 0:1]))
            P.dma(S, "bnd", lambda e: e.dma_start(out=bnd_dram[0:1, 1:2],
                                                  in_=r_rstd[0:1, 0:1]))
            P.mark("bnd_out", "bnd")
            P.wait(S, "bnd", P.val("bnd_out"))
            P.dma(S, "bnd", lambda e: e.dma_start(
                out=bndcol[:, :], in_=bnd_dram[0:1, :].partition_broadcast(128)))
            P.mark("bnd_bc", "bnd")
            P.wait(V, "bnd", P.val("bnd_bc"))
            P.do(V, lambda e: e.tensor_scalar(out=xn[:, :, 0],
                                              in0=sx[0][:, :, 0],
                                              scalar1=bndcol[:, 0:1],
                                              scalar2=bndcol[:, 1:2],
                                              op0=ALU.subtract, op1=ALU.mult),
                 [("v", 1)])
            P.mark("bnd_done_v", "v")

        for c in range(NCH):
            pa = c % 2
            if c >= 2:
                P.wait(Y, "v", max(P.val(f"apply_A{c - 2}"),
                                   P.val("bnd_done_v") if c == 2 else 0))
                P.wait(Y, "s", P.val(f"rstd_A{c - 2}"))
            for j in range(DT):
                P.dma(Y, f"sx{pa}", lambda e, j=j, c=c, pa=pa: e.dma_start(
                    out=sx[pa][:, j, 0:TC + 1],
                    in_=xin[j * 128:(j + 1) * 128, c * TC:c * TC + TC + 1]))
            P.mark(f"xA{c}", f"sx{pa}")
            emit_ln(sx[pa], 1, TC, xn, c * TC + 1, f"A{c}", xb, xb2,
                    pst, pbm, pbr,
                    v_wait=[(f"sx{pa}", P.val(f"xA{c}"))],
                    s_wait=[(f"sx{pa}", P.val(f"xA{c}"))],
                    rows_wait=([("bnd", P.val("bnd_out"))] if c == 1 else ()))
            if c == 0:
                emit_boundary()
        P.mark("xn_all", "v")
        for name, sem, wsrc, sdst in (("g", "pg", gw, scr_g),
                                      ("u", "pu", uw, scr_u)):
            for ic in range(IC):
                src3 = wsrc[:, ic * 128:(ic + 1) * 128].rearrange(
                    "(j p) e -> p j e", p=128)
                for h in range(2):
                    P.dma(G, sem, lambda e, ic=ic, h=h, src3=src3, sdst=sdst:
                          e.dma_start(out=sdst[ic][:, h * 8:(h + 1) * 8, :],
                                      in_=src3[:, h * 8:(h + 1) * 8, :]))
            P.mark(f"pro{name}", sem)
        if debug:
            P.wait(Y, "v", P.val("xn_all"))
            P.dma(Y, "dbg", lambda e: e.dma_start(out=dbg_xn, in_=xn[:, :, :]))
            P.mark("dxn", "dbg")

        for et in range(ET):
            pa = et % 2
            for pname, psem, ssem, sdst, sbuf_, mmn in (
                    ("k", "pk", f"slk{pa}", scr_k, sbk, "kmm"),
                    ("v", "pv", f"slv{pa}", scr_v, sbv, "vmm"),
                    ("r", "pr", f"slr{pa}", scr_r, sbr, "rmm")):
                P.wait(Y, psem, P.val(f"pro{pname}"))
                if et >= 2:
                    P.wait(Y, "t", P.val(f"{mmn}{et - 2}_3"))
                else:
                    P.wait(Y, "v", P.val("xn_all"))
                P.dma(Y, ssem, lambda e, et=et, pa=pa, sdst=sdst, sbuf_=sbuf_:
                      e.dma_start(out=sbuf_[pa][:, :],
                                  in_=sdst[et].rearrange("p j e -> p (j e)")))
                P.mark(f"slab_{pname}{et}", ssem)

            P.wait(T, f"slk{pa}", P.val(f"slab_k{et}"))
            if et == 0:
                P.wait(T, "v", P.val("xn_all"))
            P.wait(T, "s", P.val(f"rev{et - 1}_3") if et >= 1 else 0)
            for j in range(DT):
                for tc in range(4):
                    P.do(T, lambda e, j=j, tc=tc, pa=pa: e.matmul(
                        psK[tc][:, :], sbk[pa][:, j * 128:(j + 1) * 128],
                        xn[:, j, tc * TC:tc * TC + TC],
                        start=(j == 0), stop=(j == DT - 1)),
                        [("t", 1)] if j == DT - 1 else [])
                    if j == DT - 1:
                        P.mark(f"kmm{et}_{tc}", "t")
            P.wait(S, "v", P.val(f"dnsc{et - 2}") if et >= 2 else 0)
            for tc in range(4):
                P.wait(S, "t", P.val(f"kmm{et}_{tc}"))
                P.do(S, lambda e, tc=tc, pa=pa: e.activation(
                    out=ekb[pa][:, tc * TC:(tc + 1) * TC], in_=psK[tc][:, :],
                    func=AF.Exp), [("s", 1)])
                P.mark(f"ek{et}_{tc}", "s")
            P.wait(T, f"slv{pa}", P.val(f"slab_v{et}"))
            P.wait(T, "v", P.val(f"aa{et - 1}_3") if et >= 1 else 0)
            for j in range(DT):
                for tc in range(4):
                    P.do(T, lambda e, j=j, tc=tc, pa=pa: e.matmul(
                        psV[tc][:, :], sbv[pa][:, j * 128:(j + 1) * 128],
                        xn[:, j, tc * TC:tc * TC + TC],
                        start=(j == 0), stop=(j == DT - 1)),
                        [("t", 1)] if j == DT - 1 else [])
                    if j == DT - 1:
                        P.mark(f"vmm{et}_{tc}", "t")
            P.wait(V, f"spn{pa}", P.val(f"spn{et - 2}") if et >= 2 else 0)
            for tc in range(4):
                P.wait(V, "t", P.val(f"vmm{et}_{tc}"))
                P.wait(V, "s", P.val(f"ek{et}_{tc}"))
                P.do(V, lambda e, tc=tc, pa=pa: e.tensor_mul(
                    out=aab[pa][:, tc * TC:(tc + 1) * TC],
                    in0=ekb[pa][:, tc * TC:(tc + 1) * TC], in1=psV[tc][:, :]),
                    [("v", 1)])
                P.mark(f"aa{et}_{tc}", "v")
            P.wait(T, f"slr{pa}", P.val(f"slab_r{et}"))
            P.wait(T, "s", P.val(f"ek{et}_3"))
            for j in range(DT):
                for tc in range(4):
                    P.do(T, lambda e, j=j, tc=tc, pa=pa: e.matmul(
                        psK[tc][:, :], sbr[pa][:, j * 128:(j + 1) * 128],
                        xn[:, j, tc * TC + 1:tc * TC + TC + 1],
                        start=(j == 0), stop=(j == DT - 1)),
                        [("t", 1)] if j == DT - 1 else [])
                    if j == DT - 1:
                        P.mark(f"rmm{et}_{tc}", "t")
            if et == 0:
                P.wait(V, "s", P.val("wcol_done"))
            P.do(V, lambda e, et=et, pa=pa: e.tensor_tensor_scan(
                out=nmb[pa][:, :],
                data0=wcol[:, et:et + 1].broadcast_to([128, TCORE]),
                data1=aab[pa][:, :], initial=0.0, op0=ALU.mult, op1=ALU.add),
                [("v", 1)])
            P.mark(f"nm{et}", "v")
            P.wait(V, f"spd{pa}", P.val(f"spd{et - 2}") if et >= 2 else 0)
            P.do(V, lambda e, et=et, pa=pa: e.tensor_tensor_scan(
                out=dnb[pa][:, :],
                data0=wcol[:, et:et + 1].broadcast_to([128, TCORE]),
                data1=ekb[pa][:, :], initial=0.0, op0=ALU.mult, op1=ALU.add),
                [("v", 1)])
            P.do(V, lambda e, et=et, pa=pa: e.tensor_copy(
                out=stt[:, et:et + 1], in_=nmb[pa][:, TCORE - 1:TCORE]), [("v", 1)])
            P.mark(f"dnsc{et}", "v")
            P.wait(S, "v", P.val(f"dnsc{et}"))
            P.do(S, lambda e, pa=pa: e.activation(
                out=dnb[pa][:, :], in_=dnb[pa][:, :], func=AF.Identity,
                bias=epsb[:, 0:1]), [("s", 1)])
            P.mark(f"eps{et}", "s")
            P.wait(V, "s", P.val(f"eps{et}"))
            P.do(V, lambda e, et=et, pa=pa: e.tensor_copy(
                out=stt[:, 16 + et:17 + et], in_=dnb[pa][:, TCORE - 1:TCORE]),
                [("v", 1)])
            P.mark(f"dn{et}", "v")
            P.wait(S, f"spr{pa}", P.val(f"spr{et - 2}") if et >= 2 else 0)
            for tc in range(4):
                P.wait(S, "t", P.val(f"rmm{et}_{tc}"))
                P.do(S, lambda e, tc=tc, pa=pa: e.activation(
                    out=rrb[pa][:, tc * TC:(tc + 1) * TC], in_=psK[tc][:, :],
                    func=AF.Copy), [("s", 1)])
                if tc == 3:
                    P.mark(f"rev{et}_3", "s")
            P.wait(S, "v", P.val(f"nm{et}"))
            P.dma(S, f"spn{pa}", lambda e, et=et, pa=pa: e.dma_start(
                out=sp_num[et], in_=nmb[pa][:, :]))
            P.mark(f"spn{et}", f"spn{pa}")
            P.dma(S, f"spd{pa}", lambda e, et=et, pa=pa: e.dma_start(
                out=sp_den[et], in_=dnb[pa][:, :]))
            P.mark(f"spd{et}", f"spd{pa}")
            P.dma(S, f"spr{pa}", lambda e, et=et, pa=pa: e.dma_start(
                out=sp_r[et], in_=rrb[pa][:, :]))
            P.mark(f"spr{et}", f"spr{pa}")

        P.mark("state_all", "v")
        P.wait(Y, "v", P.val("state_all"))
        P.dma(Y, "cci", lambda e: e.dma_start(out=cc_in, in_=stt[:, :]))
        P.mark("ccin", "cci")
        P.wait(G, "cci", P.val("ccin"))
        P.do(G, lambda e: e.collective_compute(
            "AllGather", ALU.bypass,
            replica_groups=[[0, 1], [2, 3], [4, 5], [6, 7]],
            ins=[cc_in], outs=[cc_out]), [("g", 1)])
        P.mark("cc", "g")
        for hc in range(HC):
            src3 = dwn[:, hc * 128:(hc + 1) * 128].rearrange("(i p) e -> p i e", p=128)
            for h in range(6):
                P.dma(G, "pd", lambda e, hc=hc, h=h, src3=src3: e.dma_start(
                    out=scr_d[hc][:, h * 8:(h + 1) * 8, :],
                    in_=src3[:, h * 8:(h + 1) * 8, :]))
        P.mark("prod", "pd")

        P.wait(Y, "g", P.val("cc"))
        P.dma(Y, "gt", lambda e: e.dma_start(out=gath[:, :], in_=cc_out[0:128, :]))
        P.mark("gath", "gt")
        P.wait(V, "gt", P.val("gath"))
        P.do(V, lambda e: e.tensor_scalar_mul(out=ninb[:, :], in0=gath[:, 0:16],
                                              scalar1=s_cm[:, 0:1]), [("v", 1)])
        P.do(V, lambda e: e.tensor_scalar_mul(out=dinb[:, :], in0=gath[:, 16:32],
                                              scalar1=s_cm[:, 0:1]), [("v", 1)])
        P.mark("nin_done", "v")
        if debug:
            P.wait(Y, "gt", P.val("gath"))
            P.dma(Y, "dbg", lambda e: e.dma_start(out=dbg_gath, in_=gath[:, :]))

        def xB_load(c):
            if c == 0:
                for p in range(2):
                    P.wait(Y, f"spn{p}", P.val(f"spn{14 + p}"))
                    P.wait(Y, f"spd{p}", P.val(f"spd{14 + p}"))
                    P.wait(Y, f"spr{p}", P.val(f"spr{14 + p}"))
                if debug:
                    P.wait(Y, "dbg", P.val("dxn"))
            if c >= 2:
                P.wait(Y, f"ou{c % 2}", P.val(f"out{c - 2}"))
            for j in range(DT):
                P.dma(Y, f"xb{c % 2}", lambda e, j=j, c=c: e.dma_start(
                    out=sxB[c % 2][:, j, :],
                    in_=xin[j * 128:(j + 1) * 128, 1 + c * TC:1 + (c + 1) * TC]))
            P.mark(f"xB{c}", f"xb{c % 2}")

        def combine_head(c):
            P.do(V, lambda e, c=c: e.tensor_scalar_mul(
                out=nlb[:, :], in0=neglam[:, :], scalar1=float(c * TC + 1)),
                [("v", 1)])
            P.mark(f"nlb{c}", "v")

        def combine_et(c, et):
            pa = et % 2
            wp_b = wpb[0] if et % 2 == 0 else tmpf
            prev = (P.val(f"comb{c}_{et - 2}") if et >= 2 else
                    (P.val(f"comb{c - 1}_{14 + et}") if c >= 1 else 0))
            P.wait(G, "v", prev)
            for src, dstb, bsem, spsem in (
                    (sp_num, bnum, f"bn{pa}", "spn"),
                    (sp_den, bden, f"bd{pa}", "spd"),
                    (sp_r, brr, f"br{pa}", "spr")):
                P.wait(G, f"{spsem}{pa}", P.val(f"{spsem}{et}"))
                P.dma(G, bsem, lambda e, et=et, src=src, dstb=dstb, c=c, pa=pa:
                      e.dma_start(out=dstb[pa][:, :],
                                  in_=src[et][:, c * TC:(c + 1) * TC]))
                P.mark(f"{bsem}m{c}_{et}", bsem)
            prev2 = (P.val(f"comb{c}_{et - 2}") if et >= 2 else
                     (P.val(f"comb{c - 1}_{14 + et}") if c >= 1 else 0))
            P.wait(S, "v", max(prev2, P.val(f"nlb{c}")))
            P.do(S, lambda e, et=et, wp_b=wp_b: e.activation(
                out=wp_b[:, :], in_=iot[:, :], func=AF.Exp,
                scale=neglam[:, et:et + 1], bias=nlb[:, et:et + 1]),
                [("s", 1)])
            P.mark(f"wp{c}_{et}", "s")
            P.wait(V, f"bn{pa}", P.val(f"bn{pa}m{c}_{et}"))
            P.wait(V, f"bd{pa}", P.val(f"bd{pa}m{c}_{et}"))
            P.wait(V, f"br{pa}", P.val(f"br{pa}m{c}_{et}"))
            P.wait(V, "s", P.val(f"wp{c}_{et}"))
            if et == 0:
                P.wait(V, f"xb{c % 2}", P.val(f"xB{c}"))
            P.do(V, lambda e, et=et, pa=pa, wp_b=wp_b: e.scalar_tensor_tensor(
                out=bnum[pa][:, :], in0=wp_b[:, :],
                scalar=ninb[:, et:et + 1], in1=bnum[pa][:, :],
                op0=ALU.mult, op1=ALU.add), [("v", 1)])
            P.do(V, lambda e, et=et, pa=pa, wp_b=wp_b: e.scalar_tensor_tensor(
                out=bden[pa][:, :], in0=wp_b[:, :],
                scalar=dinb[:, et:et + 1], in1=bden[pa][:, :],
                op0=ALU.mult, op1=ALU.add), [("v", 1)])
            P.do(V, lambda e, pa=pa, wp_b=wp_b: e.reciprocal(out=wp_b[:, :],
                                                  in_=bden[pa][:, :]),
                 [("v", 1)])
            P.do(V, lambda e, pa=pa, wp_b=wp_b: e.tensor_mul(
                out=bnum[pa][:, :], in0=bnum[pa][:, :], in1=wp_b[:, :]),
                [("v", 1)])
            P.do(V, lambda e, pa=pa: e.tensor_mul(
                out=brr[pa][:, :], in0=brr[pa][:, :], in1=bnum[pa][:, :]),
                [("v", 1)])
            P.do(V, lambda e, et=et, pa=pa, c=c: e.tensor_add(
                out=sxB[c % 2][:, et, :], in0=sxB[c % 2][:, et, :],
                in1=brr[pa][:, :]), [("v", 1)])
            P.mark(f"comb{c}_{et}", "v")

        def combine_tail(c):
            if debug:
                P.wait(Y, "v", P.val(f"comb{c}_15"))
                P.dma(Y, "dbg", lambda e, c=c: e.dma_start(
                    out=dbg_x1[c], in_=sxB[c % 2][:, :, :]))
                P.mark(f"dx1_{c}", "dbg")

        def combine(c):
            combine_head(c)
            for et in range(ET):
                combine_et(c, et)
            combine_tail(c)

        def ln2(c):
            emit_ln(sxB[c % 2], 0, TC, xn2, 0, f"B{c}", xbB, xb2B,
                    pst2, pbm2, pbr2,
                    s_wait=[("v", P.val(f"comb{c}_15"))],
                    t_wait=([("s", P.val("rev15_3"))] if c == 0 else ()),
                    apply_wait=([("dbg", P.val(f"dxn2_{c - 1}"))]
                                if (debug and c >= 1) else ()))
            if debug:
                P.wait(Y, "v", P.val(f"apply_B{c}"))
                P.dma(Y, "dbg", lambda e, c=c: e.dma_start(
                    out=dbg_xn2[c], in_=xn2[:, :, :]))
                P.mark(f"dxn2_{c}", "dbg")

        def gate_up(c):
            for ic in range(IC):
                pb = ic % 2
                for wsem, ssem, sdst, sbuf_, mmn in (
                        ("pg", f"gs{pb}", scr_g, sgw, "gmm"),
                        ("pu", f"us{pb}", scr_u, suw, "umm")):
                    P.wait(Y, wsem, P.val(f"pro{wsem[1]}"))
                    P.wait(Y, "t", P.val(f"{mmn}{c}_{ic - 2}") if ic >= 2 else
                           (P.val(f"{mmn}{c - 1}_{IC - 2 + ic}") if c >= 1 else 0))
                    P.dma(Y, ssem, lambda e, ic=ic, pb=pb, sdst=sdst, sbuf_=sbuf_:
                          e.dma_start(out=sbuf_[pb][:, :],
                                      in_=sdst[ic].rearrange("p j e -> p (j e)")))
                    P.mark(f"{mmn}sl{c}_{ic}", ssem)
                P.wait(T, f"gs{pb}", P.val(f"gmmsl{c}_{ic}"))
                if ic == 0 and c == 0:
                    P.wait(T, "s", P.val("rev15_3"))
                P.wait(T, "v", P.val(f"fuse{c}_{ic - 2}") if ic >= 2 else
                       (P.val(f"fuse{c - 1}_{IC - 2 + ic}") if c >= 1 else 0))
                for j in range(DT):
                    if ic == 0:
                        P.wait(T, "v", P.val(f"apply_B{c}_j{j}"))
                    P.do(T, lambda e, j=j, ic=ic, pb=pb: e.matmul(
                        psG[pb][:, :], sgw[pb][:, j * 128:(j + 1) * 128],
                        xn2[:, j, :], start=(j == 0), stop=(j == DT - 1)),
                        [("t", 1)] if j == DT - 1 else [])
                P.mark(f"gmm{c}_{ic}", "t")
                P.wait(T, f"us{pb}", P.val(f"ummsl{c}_{ic}"))
                for j in range(DT):
                    P.do(T, lambda e, j=j, ic=ic, pb=pb: e.matmul(
                        psU[pb][:, :], suw[pb][:, j * 128:(j + 1) * 128],
                        xn2[:, j, :], start=(j == 0), stop=(j == DT - 1)),
                        [("t", 1)] if j == DT - 1 else [])
                P.mark(f"umm{c}_{ic}", "t")
                P.wait(S, "t", P.val(f"gmm{c}_{ic}"))
                P.wait(S, "v", P.val(f"fuse{c}_{ic - 1}") if ic >= 1 else
                       (P.val(f"fuse{c - 1}_{IC - 1}") if c >= 1 else 0))
                P.do(S, lambda e, ic=ic, pb=pb: e.activation(
                    out=tsil[0][:, :], in_=psG[pb][:, :], func=AF.Silu,
                    scale=s_gs[:, ic:ic + 1]), [("s", 1)])
                P.mark(f"silu{c}_{ic}", "s")
                P.wait(V, "t", P.val(f"umm{c}_{ic}"))
                P.wait(V, "s", P.val(f"silu{c}_{ic}"))
                P.do(V, lambda e, ic=ic, pb=pb: e.scalar_tensor_tensor(
                    out=fuse[:, ic, :], in0=psU[pb][:, :],
                    scalar=s_us[:, ic:ic + 1], in1=tsil[0][:, :],
                    op0=ALU.mult, op1=ALU.mult), [("v", 1)])
                P.mark(f"fuse{c}_{ic}", "v")

        def down(c, nxt=None):
            if nxt is not None:
                combine_head(nxt)
            for hc in range(HC):
                pb = hc % 2
                P.wait(Y, "pd", P.val("prod"))
                P.wait(Y, "t", P.val(f"dmm{c}_{hc - 2}") if hc >= 2 else
                       (P.val(f"dmm{c - 1}_{HC - 2 + hc}") if c >= 1 else 0))
                P.dma(Y, f"ds{pb}", lambda e, hc=hc, pb=pb: e.dma_start(
                    out=sdw[pb][:, :, :], in_=scr_d[hc]))
                P.mark(f"dsl{c}_{hc}", f"ds{pb}")
                P.wait(T, f"ds{pb}", P.val(f"dsl{c}_{hc}"))
                if hc == 0:
                    P.wait(T, "v", P.val(f"fuse{c}_{IC - 1}"))
                P.wait(T, "v", P.val(f"dev{c}_{hc - 1}") if hc >= 1 else
                       (P.val(f"dev{c - 1}_{HC - 1}") if c >= 1 else 0))
                for i in range(48):
                    P.do(T, lambda e, i=i, hc=hc, pb=pb: e.matmul(
                        psD[:, :], sdw[pb][:, i, :], fuse[:, i, :],
                        start=(i == 0), stop=(i == 47)),
                        [("t", 1)] if i == 47 else [])
                P.mark(f"dmm{c}_{hc}", "t")
                P.wait(V, "t", P.val(f"dmm{c}_{hc}"))
                if debug and hc == 0:
                    P.wait(V, "dbg", P.val(f"dx1_{c}"))
                P.do(V, lambda e, hc=hc, c=c: e.scalar_tensor_tensor(
                    out=sxB[c % 2][:, hc, :], in0=psD[:, :],
                    scalar=s_dsc[:, hc:hc + 1], in1=sxB[c % 2][:, hc, :],
                    op0=ALU.mult, op1=ALU.add), [("v", 1)])
                P.mark(f"dev{c}_{hc}", "v")
                if nxt is not None:
                    combine_et(nxt, hc)
            if nxt is not None:
                combine_tail(nxt)

        def out_store(c):
            P.wait(Y, "v", P.val(f"dev{c}_{HC - 1}"))
            for j in range(DT):
                P.dma(Y, f"ou{c % 2}", lambda e, j=j, c=c: e.dma_start(
                    out=xout[j * 128:(j + 1) * 128, c * TC:(c + 1) * TC],
                    in_=sxB[c % 2][:, j, :]))
            P.mark(f"out{c}", f"ou{c % 2}")

        xB_load(0)
        combine(0)
        ln2(0)
        for c in range(NCH):
            gate_up(c)
            if c + 1 < NCH:
                xB_load(c + 1)
            down(c, nxt=(c + 1 if c + 1 < NCH else None))
            if c + 1 < NCH:
                ln2(c + 1)
            out_store(c)

        P.wait(Y, "ou0", P.val("out2"))
        P.wait(Y, "ou1", P.val("out3"))

    sems = {}
    sem_es = ExitStack()
    for s in SEMS:
        sems[s] = sem_es.enter_context(nc.semaphore(s))

    marks = {}
    emit(Sch(nc, None, sems, marks))

    with nc.Block() as block:
        @block.sync
        def _(sync):
            emit(Sch(nc, "sync", sems, marks))

        @block.gpsimd
        def _(gpsimd):
            emit(Sch(nc, "gpsimd", sems, marks))

        @block.tensor
        def _(tensor):
            emit(Sch(nc, "tensor", sems, marks))

        @block.vector
        def _(vector):
            emit(Sch(nc, "vector", sems, marks))

        @block.scalar
        def _(scalar):
            emit(Sch(nc, "scalar", sems, marks))

    sem_es.close()
    es.close()
    return nc


_NC_CACHE = None


def kernel(L0_attn_x, L0_attn_Wk, L0_attn_Wv, L0_attn_Wr, L0_attn_time_decay,
           L0_ffn_gate_w, L0_ffn_up_w, L0_ffn_down_w, L0_ffn_gate_scale,
           L0_ffn_up_scale, L0_ffn_down_scale):
    global _NC_CACHE
    x = np.asarray(L0_attn_x, dtype=np.float32)
    B, T, Dm = x.shape
    assert (B, T, Dm) == (4, 4096, 2048)

    wkTh = np.ascontiguousarray(np.asarray(L0_attn_Wk, np.float32).T)
    wvTh = np.ascontiguousarray(np.asarray(L0_attn_Wv, np.float32).T)
    wrTh = np.ascontiguousarray(np.asarray(L0_attn_Wr, np.float32).T)
    gwh = np.ascontiguousarray(np.asarray(L0_ffn_gate_w, np.float32))
    uwh = np.ascontiguousarray(np.asarray(L0_ffn_up_w, np.float32))
    dwh = np.ascontiguousarray(np.asarray(L0_ffn_down_w, np.float32))
    td_colh = np.ascontiguousarray(
        np.asarray(L0_attn_time_decay, np.float32).reshape(16, 128).T)
    gs_colh = np.ascontiguousarray(
        np.asarray(L0_ffn_gate_scale, np.float32).reshape(48, 128).T)
    us_colh = np.ascontiguousarray(
        np.asarray(L0_ffn_up_scale, np.float32).reshape(48, 128).T)
    ds_colh = np.ascontiguousarray(
        np.asarray(L0_ffn_down_scale, np.float32).reshape(16, 128).T)

    in_maps = []
    for c in range(8):
        b, h = c // 2, c % 2
        t0 = h * TCORE
        xinh = np.empty((Dm, TCORE + 1), np.float32)
        if h == 0:
            xinh[:, 0] = 0.0
        else:
            xinh[:, 0] = x[b, t0 - 1]
        xinh[:, 1:] = x[b, t0:t0 + TCORE].T
        cm = np.full((128, 1), 1.0 if h == 1 else 0.0, np.float32)
        in_maps.append(dict(xin=np.ascontiguousarray(xinh), wkT=wkTh, wvT=wvTh,
                            wrT=wrTh, gw=gwh, uw=uwh, dwn=dwh, td_col=td_colh,
                            gs_col=gs_colh, us_col=us_colh, dsc_col=ds_colh,
                            cmask=cm))

    if _NC_CACHE is None:
        _NC_CACHE = build_program()
    res = run_bass_kernel_spmd(_NC_CACHE, in_maps, core_ids=list(range(8)))

    out = np.empty((B, T, Dm), np.float32)
    for c in range(8):
        b, h = c // 2, c % 2
        t0 = h * TCORE
        out[b, t0:t0 + TCORE] = res.results[c]["xout"].T
    return out
